# version 06 (final)
# baseline (speedup 1.0000x reference)
import numpy as np
from contextlib import ExitStack

import concourse.bass as bass
import concourse.mybir as mybir
from concourse.bass_utils import run_bass_kernel_spmd

F32 = mybir.dt.float32
BF16 = mybir.dt.bfloat16
AF = mybir.ActivationFunctionType
ALU = mybir.AluOpType
AXX = mybir.AxisListType.X

D = 2048
TOWN = 1024
NTT = 8
S = 2048
FC = 16
SF = 22
EF = 11
NE = 8
NOF = 4
NG = 4
CH = 64
NCH = 32

_TAPS = ()


class Sem:
    def __init__(self, ctx, name):
        self.h = ctx.es.enter_context(ctx.nc.semaphore(name))
        self.n = 0

    def inc(self, inst, k=1):
        inst.then_inc(self.h, k)
        self.n += k
        return inst


class Ctx:
    def __init__(self, nc, es, block, taps):
        self.nc = nc
        self.es = es
        self.block = block
        self.taps = set(taps)
        self._sems = {}
        self.tap_shapes = {}

    def sem(self, name):
        if name not in self._sems:
            self._sems[name] = Sem(self, name)
        return self._sems[name]

    def fence(self, eng, inst, name):
        f = self.sem(name)
        f.inc(inst, 1)
        eng.wait_ge(f.h, f.n)

    def sb(self, name, cols_or_shape, dtype, es=None):
        shape = cols_or_shape if isinstance(cols_or_shape, list) else [128, cols_or_shape]
        t = (es or self.es).enter_context(self.nc.sbuf_tensor(name, shape, dtype))
        import os
        if os.environ.get("SBUF_DEBUG"):
            print(f"SBUF {name:12s} {str(shape):16s} rem={self.nc.sbuf_bytes_remaining}")
        return t


class Banks:

    def __init__(self, ctx):
        self.ctx = ctx
        self.t = [ctx.es.enter_context(ctx.nc.psum_tensor(f"pb{i}", [128, 512], F32))
                  for i in range(8)]
        self.sem = [ctx.sem(f"pbs{i}") for i in range(8)]
        self.need = [0] * 8

    def f32(self, i, cols=512):
        return self.t[i][:, 0:cols]

    def bf16(self, i, cols=1024):
        return self.t[i][:, :].bitcast(BF16)[:, 0:cols]

    def acquire(self, eng, i):
        if self.need[i] > 0:
            eng.wait_ge(self.sem[i].h, self.need[i])
        self.need[i] += 1

    def done(self, inst, i):
        self.sem[i].inc(inst, 1)


class WStream:

    def __init__(self, ctx, name, cols, nslots=2, es=None, sems=None):
        self.ctx = ctx
        self.nslots = nslots
        self.slots = [ctx.sb(f"{name}{i}", cols, BF16, es) for i in range(nslots)]
        self.loaded = ctx.sem(f"{sems or name}_ld")
        self.free = ctx.sem(f"{sems or name}_fr")
        self.free0 = self.free.n
        self.fills = 0
        self.tgt = []
        self.acq = 0

    def fill(self, g, src_ap, off=0, ncols=None):
        slot = self.slots[self.fills % self.nslots]
        if self.fills >= self.nslots:
            g.wait_ge(self.free.h, self.free0 + self.fills - self.nslots + 1)
        ncols = ncols if ncols is not None else src_ap.shape[-1]
        dst = slot[:, off:off + ncols]
        self.loaded.inc(g.dma_start(out=dst, in_=src_ap), 16)
        self.fills += 1
        self.tgt.append(self.loaded.n)
        return slot

    def acquire(self, eng):
        eng.wait_ge(self.loaded.h, self.tgt[self.acq])
        slot = self.slots[self.acq % self.nslots]
        self.acq += 1
        return slot

    def release(self, inst, k=1):
        self.free.inc(inst, k)


def _r3(ap):
    return ap.rearrange("a b c -> b a c")



def phase_layernorm(ctx, resid, xh, tag, es, extra_sink=None):
    bk = ctx.banks
    xc = [ctx.sb(f"{tag}_xc{i}", D, F32, es) for i in range(2)]
    sq = ctx.sb(f"{tag}_sq", D, F32, es)
    xtm = [ctx.sb(f"{tag}_xtm{i}", D, BF16, es) for i in range(2)]
    sm = ctx.sb(f"{tag}_sm", 4 * NTT, F32, es)
    vml = ctx.sem("ln_vml")
    aml = ctx.sem("ln_aml")
    xt = ctx.sem("ln_xt")
    tp = ctx.sem("ln_tp")
    sx = ctx.sem("ln_sx")
    tp0 = tp.n

    for t in range(NTT):
        mean = sm[:, 4 * t:4 * t + 1]
        ssq = sm[:, 4 * t + 1:4 * t + 2]
        rstd = sm[:, 4 * t + 2:4 * t + 3]
        rt = resid[:, t * D:(t + 1) * D]
        xct = xc[t % 2]
        xtmt = xtm[t % 2]

        def _v(v, mean=mean, ssq=ssq, rt=rt, xct=xct):
            ctx.fence(v, v.reduce_sum(mean, rt, axis=AXX), "fv")
            ctx.fence(v, v.tensor_scalar(mean, mean, 1.0 / D, None, ALU.mult), "fv")
            v.tensor_scalar(xct[:, :], rt, mean, None, ALU.subtract)
            ctx.fence(v, v.scalar_tensor_tensor(sq[:, :], xct[:, :], 1.0, xct[:, :],
                                                ALU.bypass, ALU.mult, accum_out=ssq),
                      "fv")
            vml.inc(v.tensor_scalar(ssq, ssq, 1.0 / D, 1e-5, ALU.mult, ALU.add), 1)
        ctx.block.vector(_v)
        vml_t = vml.n

        def _a(sc_, ssq=ssq, rstd=rstd, vml_t=vml_t):
            sc_.wait_ge(vml.h, vml_t)
            ctx.fence(sc_, sc_.activation(rstd, ssq, AF.Ln), "fa")
            aml.inc(sc_.activation(rstd, rstd, AF.Exp, scale=-0.5), 1)
        ctx.block.scalar(_a)
        aml_t = aml.n

        def _v2(v, t=t, rstd=rstd, xct=xct, xtmt=xtmt, aml_t=aml_t):
            v.wait_ge(aml.h, aml_t)
            if t >= 2:
                v.wait_ge(tp.h, tp0 + FC * (t - 1))
            xt.inc(v.tensor_scalar(xtmt[:, :], xct[:, :], rstd, None, ALU.mult), 1)
        ctx.block.vector(_v2)
        xt_t = xt.n

        for half in range(2):
            def _p(tt, half=half, xtmt=xtmt, xt_t=xt_t):
                if half == 0:
                    tt.wait_ge(xt.h, xt_t)
                for fc8 in range(8):
                    fc = half * 8 + fc8
                    bk.acquire(tt, fc8)
                    tp.inc(tt.transpose(bk.bf16(fc8, 128),
                                        xtmt[:, fc * 128:(fc + 1) * 128],
                                        ctx.identb[:, :]), 1)
            ctx.block.tensor(_p)
            tp_base = tp.n - 8

            def _a2(sc_, t=t, half=half, tp_base=tp_base):
                for fc8 in range(8):
                    fc = half * 8 + fc8
                    sc_.wait_ge(tp.h, tp_base + fc8 + 1)
                    dst = xh[:, fc * TOWN + t * 128: fc * TOWN + (t + 1) * 128]
                    inst = sc_.copy(dst, bk.bf16(fc8, 128))
                    if extra_sink is not None:
                        extra_sink(sc_, t, fc, bk.bf16(fc8, 128))
                    bk.done(inst, fc8)
                if t == NTT - 1 and half == 1:
                    sx.inc(sc_.nop(), 1)
            ctx.block.scalar(_a2)
    return sx, sx.n



def phase_router(ctx, xh, xh_sem, xh_val, comb, scores, sm, tag):
    bk = ctx.banks
    rp = ctx.sem("rt_rp")
    rn = ctx.sem("rt_rn")
    rp0 = rp.n

    for t in range(NTT):
        b = t % 2
        negm = sm[:, 8 * t:8 * t + 1]
        ssum = sm[:, 8 * t + 1:8 * t + 2]
        rs = sm[:, 8 * t + 2:8 * t + 3]
        m1 = sm[:, 8 * t + 3:8 * t + 4]
        m2 = sm[:, 8 * t + 4:8 * t + 5]
        sc = scores[:, t * 8:(t + 1) * 8]
        cmb = comb[:, t * 8:(t + 1) * 8]

        def _p(tt, t=t, b=b):
            if t == 0:
                tt.wait_ge(xh_sem.h, xh_val)
                tt.wait_ge(ctx.sem("wrld").h, 16)
            bk.acquire(tt, b)
            for fc in range(FC):
                inst = tt.matmul(bk.f32(b, 8),
                                 xh[:, fc * TOWN + t * 128: fc * TOWN + (t + 1) * 128],
                                 ctx.wrs[:, fc * 8:(fc + 1) * 8],
                                 start=(fc == 0), stop=(fc == FC - 1))
            rp.inc(inst, 1)
        ctx.block.tensor(_p)
        rp_t = rp.n

        def _v(v, b=b, negm=negm, rp_t=rp_t):
            v.wait_ge(rp.h, rp_t)
            rn.inc(v.reduce_max(negm, bk.f32(b, 8), axis=AXX, negate=True), 1)
        ctx.block.vector(_v)
        rn_t = rn.n

        def _a(s_, b=b, sc=sc, negm=negm, ssum=ssum, rn_t=rn_t):
            s_.wait_ge(rn.h, rn_t)
            inst = s_.activation(sc, bk.f32(b, 8), AF.Exp, bias=negm, accum_out=ssum)
            bk.done(inst, b)
        ctx.block.scalar(_a)
        bs_t = bk.sem[b].n

        def _v2(v, b=b, sc=sc, cmb=cmb, ssum=ssum, rs=rs, m1=m1, m2=m2, bs_t=bs_t):
            v.wait_ge(bk.sem[b].h, bs_t)
            ctx.fence(v, v.reciprocal(rs, ssum), "fv")
            ctx.fence(v, v.tensor_scalar(sc, sc, rs, None, ALU.mult), "fv")
            ctx.fence(v, v.reduce_max(m1, sc, axis=AXX), "fv")
            ctx.fence(v, v.tensor_scalar(cmb, sc, m1, None, ALU.is_equal), "fv")
            ctx.fence(v, v.scalar_tensor_tensor(cmb, cmb, -1e9, sc,
                                                ALU.mult, ALU.add), "fv")
            ctx.fence(v, v.reduce_max(m2, cmb, axis=AXX), "fv")
            ctx.fence(v, v.tensor_scalar(cmb, sc, m2, None, ALU.is_ge), "fv")
            v.scalar_tensor_tensor(cmb, sc, 0.5, cmb, ALU.mult, ALU.mult)
        ctx.block.vector(_v2)



def ffn_down(ctx, h, resid, dstream, nchunk, nfills_per_of, hstride,
             hready, hready_val, comb_col, tag):
    bk = ctx.banks
    sd = ctx.sem("fw_d")
    d0 = sd.n

    def _p(tt):
        for of in range(NOF):
            slots = [dstream.acquire(tt) for _ in range(nfills_per_of)]
            if of == 0:
                tt.wait_ge(hready.h, hready_val)
            for t in range(NTT):
                s2, t4 = divmod(t, 4)
                b = 4 + (t % 4)
                bk.acquire(tt, b)
                for ch in range(nchunk):
                    sl = slots[ch * nfills_per_of // nchunk]
                    chl = ch - (nchunk // nfills_per_of) * (ch * nfills_per_of // nchunk)
                    inst = tt.matmul(bk.f32(b),
                                     h[:, ch * 512 + s2 * hstride + t4 * 128:
                                       ch * 512 + s2 * hstride + (t4 + 1) * 128],
                                     sl[:, chl * 512:(chl + 1) * 512],
                                     start=(ch == 0), stop=(ch == nchunk - 1))
                sd.inc(inst, 1)
    ctx.block.tensor(_p)

    def _v(v):
        k = 0
        for of in range(NOF):
            for t in range(NTT):
                b = 4 + (t % 4)
                k += 1
                v.wait_ge(sd.h, d0 + k)
                rsl = resid[:, t * D + of * 512: t * D + of * 512 + 512]
                sc = 0.5 if comb_col is None else comb_col(t)
                inst = v.scalar_tensor_tensor(rsl, bk.f32(b), sc, rsl,
                                              ALU.mult, ALU.add)
                bk.done(inst, b)
            dstream.release(v.nop(), nfills_per_of)
    ctx.block.vector(_v)
    return sd, sd.n


def phase_ffn(ctx, resid, xh, xh_sem, xh_val, tag):
    es = ExitStack()
    h = ctx.sb(f"{tag}h_", SF * 1024, BF16, es)
    comb = ctx.sb(f"{tag}cb", 64, F32, es)
    scores = ctx.sb(f"{tag}sc", 64, F32, es)
    sm = ctx.sb(f"{tag}sm", 64, F32, es)
    gstream = WStream(ctx, f"{tag}gs", FC * 128, es=es, sems="gs")
    ustream = WStream(ctx, f"{tag}us", FC * 128, es=es, sems="us")
    dstream = WStream(ctx, f"{tag}ds", EF * 512, es=es, sems="ds")

    with es:
        def _g(g):
            fg = getattr(ctx, "ffn_fill_guard", None)
            if fg is not None:
                g.wait_ge(fg[0].h, fg[1])
            if tag == "f1":
                ctx.sem("wrld").inc(
                    g.dma_start(out=ctx.wrs[:, :],
                                in_=ctx.inp["wr"].ap().rearrange("p a c -> p (a c)")), 16)
            for sf in range(SF):
                gstream.fill(g, ctx.inp["wsg"][sf].rearrange("p a c -> p (a c)"))
                ustream.fill(g, ctx.inp["wsu"][sf].rearrange("p a c -> p (a c)"))
            for of in range(NOF):
                dstream.fill(g, ctx.inp["wsd"][of][:, 0:EF, :]
                             .rearrange("p a c -> p (a c)"))
                dstream.fill(g, ctx.inp["wsd"][of][:, EF:SF, :]
                             .rearrange("p a c -> p (a c)"))
            for e in range(NE):
                for ef in range(EF):
                    gstream.fill(g, ctx.inp["weg"][e, ef]
                                 .rearrange("p a c -> p (a c)"))
                    ustream.fill(g, ctx.inp["weu"][e, ef]
                                 .rearrange("p a c -> p (a c)"))
                for of in range(NOF):
                    dstream.fill(g, ctx.inp["wed"][e, of]
                                 .rearrange("p a c -> p (a c)"))
        ctx.block.gpsimd(_g)

        phase_router(ctx, xh, xh_sem, xh_val, comb, scores, sm, tag)
        if f"{tag}comb" in ctx.taps:
            mil = ctx.sem("mil")
            def _vm(v):
                mil.inc(v.engine_nop(), 1)
            ctx.block.vector(_vm)
            emit_tap_sbuf(ctx, f"{tag}comb", comb[:, :], (128, 64), mil, mil.n, F32)
            emit_tap_sbuf(ctx, f"{tag}scores", scores[:, :], (128, 64), mil, mil.n, F32)

        guard = getattr(ctx, "h_guard", None)
        hs, hv = ffn_gateup_guarded(ctx, xh, h, gstream, ustream, SF, SF * 512,
                                    f"{tag}S", *(guard or (None, 0)))
        prev_sd, prev_val = ffn_down(ctx, h, resid, dstream, SF, 2, SF * 512,
                                     hs, hv, None, f"{tag}S")

        for e in range(NE):
            hs, hv = ffn_gateup_guarded(ctx, xh, h, gstream, ustream, EF, EF * 512,
                                        f"{tag}e{e}", prev_sd, prev_val)
            prev_sd, prev_val = ffn_down(
                ctx, h, resid, dstream, EF, 1, EF * 512, hs, hv,
                (lambda t, e=e: comb[:, t * 8 + e: t * 8 + e + 1]), f"{tag}e{e}")
    ctx.h_guard = (prev_sd, prev_val)
    return prev_sd, prev_val


def ffn_gateup_guarded(ctx, xh, h, gstream, ustream, nchunk, hstride, tag,
                       guard_sem, guard_val):
    bk = ctx.banks
    sg = ctx.sem("fw_g")
    su = ctx.sem("fw_u")
    g0, u0 = sg.n, su.n
    silu_banksem = {}

    def _p(tt):
        for ch in range(nchunk):
            gs = gstream.acquire(tt)
            us = ustream.acquire(tt)
            for s2 in range(2):
                bg = (2 * ch + s2) % 2
                bu = 2 + bg
                bk.acquire(tt, bg)
                bk.acquire(tt, bu)
                for fc in range(FC):
                    rhs = xh[:, fc * TOWN + s2 * 512: fc * TOWN + s2 * 512 + 512]
                    ig = tt.matmul(bk.f32(bg), gs[:, fc * 128:(fc + 1) * 128],
                                   rhs, start=(fc == 0), stop=(fc == FC - 1))
                    iu = tt.matmul(bk.f32(bu), us[:, fc * 128:(fc + 1) * 128],
                                   rhs, start=(fc == 0), stop=(fc == FC - 1))
                sg.inc(ig, 1)
                su.inc(iu, 1)
    ctx.block.tensor(_p)

    def _a(s_):
        if guard_sem is not None:
            s_.wait_ge(guard_sem.h, guard_val)
        k = 0
        for ch in range(nchunk):
            for s2 in range(2):
                bg = (2 * ch + s2) % 2
                k += 1
                s_.wait_ge(sg.h, g0 + k)
                hcol = ch * 512 + s2 * hstride
                inst = s_.activation(h[:, hcol:hcol + 512], bk.f32(bg), AF.Silu)
                bk.done(inst, bg)
                silu_banksem[(ch, s2)] = (bg, bk.sem[bg].n)
            gstream.release(s_.nop(), 1)
    ctx.block.scalar(_a)

    last = None

    def _v(v):
        nonlocal last
        if guard_sem is not None:
            v.wait_ge(guard_sem.h, guard_val)
        k = 0
        for ch in range(nchunk):
            for s2 in range(2):
                bu = 2 + ((2 * ch + s2) % 2)
                k += 1
                v.wait_ge(su.h, u0 + k)
                sb, sv = silu_banksem[(ch, s2)]
                v.wait_ge(bk.sem[sb].h, sv)
                hcol = ch * 512 + s2 * hstride
                inst = v.tensor_tensor(h[:, hcol:hcol + 512], bk.f32(bu),
                                       h[:, hcol:hcol + 512], ALU.mult)
                bk.done(inst, bu)
                last = (bu, bk.sem[bu].n)
            ustream.release(v.nop(), 1)
    ctx.block.vector(_v)
    return bk.sem[last[0]], last[1]




class AttnBuilder:

    def __init__(self, ctx, es_outer, es_g, d_x1, cc_xh_src, x1_dumped):
        self.ctx = ctx
        self.d_x1 = d_x1
        self.cc_xh_src = cc_xh_src
        self.x1_dumped = x1_dumped
        nc = ctx.nc
        self.d_sc = nc.dram_tensor("d_sc", [16, NCH, 2, CH], F32)
        self.cc_xh_dst = nc.dram_tensor("cc_xh_dst", [2 * FC, 128, TOWN], BF16)
        self.cc_xn_src = nc.dram_tensor("cc_xn_src", [16, 128, TOWN], BF16)
        self.cc_xn_dst = nc.dram_tensor("cc_xn_dst", [32, 128, TOWN], BF16)

        self.d_xn = nc.dram_tensor("d_xn", [16, 128, TOWN], BF16)
        self.d_xnu = nc.dram_tensor("d_xnu", [4, 128, S], BF16)
        sb = ctx.sb
        self.XS = sb("at_xs", FC * 512, BF16, es_g)
        self.ST = sb("at_st", 8 * 2051 + 5, BF16, es_g)
        self.CV = sb("at_cv", 512, F32, es_g)
        self.ZT = sb("at_zt", 16 * 512, BF16, es_g)
        self.QSQ = sb("at_qsq", 512, BF16, es_g)
        self.BA = sb("at_ba", 16 * 32, F32, es_g)
        self.CL = [sb(f"at_cl{i}", 64, F32, es_g) for i in range(8)]
        self.CPL = sb("at_cpl", 64, F32, es_g)
        self.NEA = sb("at_nea", 16, F32, es_g)
        self.DTB = sb("at_dtb", 16, F32, es_g)
        self.TSTG = sb("at_tstg", 16, F32, es_g)
        self.RTT = sb("at_rtt", [128, S], F32, es_g)
        self.EGLROW = sb("at_eglr", [1, 128], F32, es_g)
        self.EGLB = sb("at_eglb", 128, F32, es_g)
        self.ONESR = sb("at_ones", [1, 128], F32, es_g)
        self.ONESB = sb("at_onesb", [128, 128], BF16, es_g)
        self.CMASK = sb("at_cmask", 128, F32, es_g)
        self.CTRIU = sb("at_ctriu", 128, F32, es_g)
        self.CEND = sb("at_cend", 128, F32, es_g)
        self.IDF = sb("at_idf", 128, F32, es_g)
        self.CW = sb("at_cw", 32, F32, es_g)
        self.ROW = sb("at_row", [1, 512], F32, es_g)
        self.RWB = sb("at_rwb", [1, 512], BF16, es_g)
        self.XNU = sb("at_xnu", 4 * S, BF16, es_g)
        self.TCH = [sb(f"at_tch{v}", 7 * 64, BF16, es_g) for v in range(4)]
        self.ATN = [sb(f"at_atn{v}", 64, BF16, es_g) for v in range(4)]
        self.NTV = [sb(f"at_ntv{v}", 64, BF16, es_g) for v in range(4)]
        self.KBE = [sb(f"at_kbe{v}", 128, BF16, es_g) for v in range(4)]
        self.KGL = [sb(f"at_kgl{v}", 128, BF16, es_g) for v in range(4)]
        self.VBT = [sb(f"at_vb{v}", 128, BF16, es_g) for v in range(4)]
        self.QET = [sb(f"at_qet{v}", 128, BF16, es_g) for v in range(4)]
        self.QEF = [sb(f"at_qef{v}", 64, BF16, es_g) for v in range(4)]
        self.KCDN = [sb(f"at_kcdn{v}", 64, BF16, es_g) for v in range(4)]
        self.VPRN = [sb(f"at_vprn{v}", 128, F32, es_g) for v in range(4)]
        self.SIX = [sb(f"at_si{v}", 128, F32, es_g) for v in range(4)]
        self.GTX = [sb(f"at_gt{v}", 128, F32, es_g) for v in range(4)]
        self.VNB = [sb(f"at_vnb{v}", 128, BF16, es_g) for v in range(4)]
        self.SST = [sb(f"at_s{v}", 128, F32, es_g) for v in range(4)]
        self.SBF = [sb(f"at_sbf{v}", 128, BF16, es_g) for v in range(4)]
        self.EBX = [sb(f"at_eb{v}", 16 * 128, BF16, es_g) for v in range(4)]
        self.DTX = sb("at_dx", 8 * 128, F32, es_g)
        self.SMA = sb("at_sma", 64, F32, es_g)
        self.POFS = sb("at_pofs", [1, 12], mybir.dt.int32, es_g)
        self.XNTM = [sb(f"at_xntm{v}", 128, BF16, es_g) for v in range(4)]

        self.aw = WStream(ctx, "at_aw", 8 * FC * 128, nslots=1, es=es_g)
        self.zw = WStream(ctx, "at_zw", FC * 512, nslots=1, es=es_g)
        self.bw = WStream(ctx, "at_bw", FC * 32, nslots=1, es=es_g)

        self.sxs = ctx.sem("at_xs_")
        self.scc = ctx.sem("at_cc")
        self.sdx = ctx.sem("at_dx_")
        self.pm = ctx.sem("at_pm")
        self.vm = ctx.sem("at_vm")
        self.am = ctx.sem("at_am")
        self.sg2 = ctx.sem("at_g2")
        self.stg = ctx.sem("at_stg")
        self.fv = ctx.sem("fv")
        self.fa = ctx.sem("fa")
        self.xs_free = 0
        self.aw_slot = None
        self.zw_slot = None
        self.bw_slot = None

    def stv(self, cc, lo, n):
        base = cc * 2051 + 3
        return self.ST[:, base + lo: base + lo + n]

    def stv_pad(self, cc, lo, n):
        base = cc * 2051
        return self.ST[:, base + lo: base + lo + n]

    def eslice(self, vg, n):
        par = (n % 2) * 64
        return self.EBX[vg][par:par + 64, (n // 2) * 128:(n // 2) * 128 + 128]

    def prologue(self):
        ctx, bk = self.ctx, self.ctx.banks
        nc = ctx.nc
        sdx_xh = self.x1_dumped
        self.rown = nc.gpsimd.alloc_register("at_rown")
        self.rpar = nc.gpsimd.alloc_register("at_rpar")
        self.rxpn = nc.gpsimd.alloc_register("at_rxpn")
        self.rtmp = nc.gpsimd.alloc_register("at_rtmp")
        self.rtmp2 = nc.gpsimd.alloc_register("at_rtmp2")

        def _g(g):
            g.wait_ge(ctx.h_guard[0].h, ctx.h_guard[1])
            g.wait_ge(self.sdx.h, self.x1_dumped)
            self.sg2.inc(g.dma_start(out=self.POFS[0:1, :],
                                     in_=ctx.inp["pofs"][0:1, :]), 16)
            self.sg2.inc(g.dma_start(out=self.NEA[0:1, 0:16],
                                     in_=ctx.inp["alog"][0:1, :]), 16)
            self.sg2.inc(g.dma_start(out=self.DTB[0:1, 0:16],
                                     in_=ctx.inp["dtb"][0:1, :]), 16)
            self.sg2.inc(g.dma_start(out=self.CMASK[:, :],
                                     in_=ctx.inp["cmask"][:, :]), 16)
            self.sg2.inc(g.dma_start(out=self.CTRIU[:, :],
                                     in_=ctx.inp["ctriu"][:, :]), 16)
            self.sg2.inc(g.dma_start(out=self.CEND[:, :],
                                     in_=ctx.inp["cend"][:, :]), 16)
            self.sg2.inc(g.dma_start(out=self.IDF[:, :],
                                     in_=ctx.inp["cident"][:, :]), 16)
            g.wait_ge(self.sdx.h, sdx_xh)
            g.wait_ge(self.sg2.h, 16)
            g.reg_load(self.rown, self.POFS[0:1, 0:1])
            g.reg_load(self.rpar, self.POFS[0:1, 1:2])
            g.reg_load(self.rxpn, self.POFS[0:1, 2:3])
            self.scc.inc(g.collective_compute(
                "AllGather", ALU.bypass,
                ins=[self.cc_xh_src.ap().opt()], outs=[self.cc_xh_dst.ap().opt()],
                replica_groups=[[0, 1], [2, 3], [4, 5], [6, 7]]), 1)
        ctx.block.gpsimd(_g)
        self.cc_xh_done = self.scc.n
        self.const_ok = self.sg2.n

        def _v0(v):
            v.wait_ge(self.sdx.h, self.x1_dumped)
            v.memset(self.ONESR[:, :], 1.0)
            v.memset(self.ONESB[:, :], 1.0)
            v.memset(self.RTT[:, :], 0.0)
            self.fv.inc(v.engine_nop(), 1)
        ctx.block.vector(_v0)
        self.onesr_ok = self.fv.n

        def _p0(tt):
            tt.wait_ge(self.sg2.h, self.const_ok)
            tt.wait_ge(self.fv.h, self.onesr_ok)
            bk.acquire(tt, 6)
            self.pm.inc(tt.matmul(bk.f32(6, 16), self.ONESR[0:1, :],
                                  self.NEA[0:1, 0:16], start=True, stop=True), 1)
            bk.acquire(tt, 7)
            self.pm.inc(tt.matmul(bk.f32(7, 16), self.ONESR[0:1, :],
                                  self.DTB[0:1, 0:16], start=True, stop=True), 1)
        ctx.block.tensor(_p0)
        pmv = self.pm.n

        def _a0(s_):
            s_.wait_ge(self.pm.h, pmv)
            inst = s_.activation(self.NEA[:, 0:16], bk.f32(6, 16), AF.Exp)
            bk.done(inst, 6)
            inst = s_.copy(self.DTB[:, 0:16], bk.f32(7, 16))
            bk.done(inst, 7)
            self.fa.inc(s_.nop(), 1)
        ctx.block.scalar(_a0)
        fav = self.fa.n
        b6v = bk.sem[6].n

        def _v0b(v):
            v.wait_ge(self.fa.h, fav)
            v.wait_ge(bk.sem[6].h, b6v)
            self.vm.inc(v.tensor_scalar(self.NEA[:, 0:16], self.NEA[:, 0:16],
                                        -1.0, None, ALU.mult), 1)
        ctx.block.vector(_v0b)
        self.nea_ok = self.vm.n

    def slab_loop(self, g):
        ctx, bk = self.ctx, self.ctx.banks

        def _gf(gp, g=g):
            self.aw.fill(gp, ctx.inp["wqkv"][g].rearrange("p a c -> p (a c)"))
            self.zw.fill(gp, ctx.inp["wz"][g].rearrange("p a c -> p (a c)"))
            if g == 0:
                self.bw.fill(gp, ctx.inp["wba"].ap().rearrange("p a c -> p (a c)"))
            self.sg2.inc(gp.dma_start(
                out=self.CW[:, :],
                in_=ctx.inp["wconv"][g].rearrange("p a c -> p (a c)")), 16)
        ctx.block.gpsimd(_gf)
        self.cw_ok = self.sg2.n

        for slab in range(4):
            half = slab // 2
            t0 = (slab % 2) * 512

            def _s(sync, g=g, slab=slab, half=half, t0=t0):
                if g == 0 and slab == 0:
                    sync.wait_ge(self.scc.h, self.cc_xh_done)
                if self.xs_free:
                    sync.wait_ge(self.pm.h, self.xs_free)
                self.sxs.inc(sync.dma_start(
                    out=self.XS[:, :].rearrange("p (f t) -> p f t", f=FC),
                    in_=self.cc_xh_dst[half * FC:(half + 1) * FC, :, t0:t0 + 512]
                    .rearrange("f p t -> p f t")), 16)
            ctx.block.sync(_s)
            sxs_t = self.sxs.n
            pm_base = self.pm.n

            def _p(tt, g=g, slab=slab, sxs_t=sxs_t):
                tt.wait_ge(self.sxs.h, sxs_t)
                if slab == 0:
                    self.aw_slot = self.aw.acquire(tt)
                    self.zw_slot = self.zw.acquire(tt)
                    if g == 0:
                        self.bw_slot = self.bw.acquire(tt)
                for cc in range(8):
                    bk.acquire(tt, cc)
                    for fc in range(FC):
                        inst = tt.matmul(
                            bk.f32(cc),
                            self.aw_slot[:, (cc * FC + fc) * 128:(cc * FC + fc + 1) * 128],
                            self.XS[:, fc * 512:(fc + 1) * 512],
                            start=(fc == 0), stop=(fc == FC - 1))
                    self.pm.inc(inst, 1)
                for tt4 in range(4):
                    b = tt4 % 2
                    bk.acquire(tt, b)
                    for fc in range(FC):
                        inst = tt.matmul(
                            bk.f32(b),
                            self.XS[:, fc * 512 + tt4 * 128: fc * 512 + (tt4 + 1) * 128],
                            self.zw_slot[:, fc * 512:(fc + 1) * 512],
                            start=(fc == 0), stop=(fc == FC - 1))
                    self.pm.inc(inst, 1)
                if g == 0:
                    for tt4 in range(4):
                        b = 2 + tt4 % 2
                        bk.acquire(tt, b)
                        for fc in range(FC):
                            inst = tt.matmul(
                                bk.f32(b, 32),
                                self.XS[:, fc * 512 + tt4 * 128: fc * 512 + (tt4 + 1) * 128],
                                self.bw_slot[:, fc * 32:(fc + 1) * 32],
                                start=(fc == 0), stop=(fc == FC - 1))
                        self.pm.inc(inst, 1)
            ctx.block.tensor(_p)
            self.xs_free = self.pm.n
            if slab == 3:
                def _rl(v):
                    v.wait_ge(self.pm.h, self.xs_free)
                    self.aw.release(v.engine_nop(), 1)
                    self.zw.release(v.engine_nop(), 1)
                ctx.block.vector(_rl)

            def _a(s_, g=g, slab=slab, pm_base=pm_base):
                zg = getattr(self, "zt_guard", 0)
                if zg and slab == 0:
                    s_.wait_ge(self.vm.h, zg)
                for cc in range(8):
                    s_.wait_ge(self.pm.h, pm_base + cc + 1)
                    inst = s_.copy(self.stv(cc, slab * 512, 512), bk.f32(cc))
                    bk.done(inst, cc)
                for tt4 in range(4):
                    b = tt4 % 2
                    s_.wait_ge(self.pm.h, pm_base + 8 + tt4 + 1)
                    inst = s_.activation(
                        self.ZT[:, (slab * 4 + tt4) * 512:(slab * 4 + tt4 + 1) * 512],
                        bk.f32(b), AF.Silu)
                    bk.done(inst, b)
                if g == 0:
                    for tt4 in range(4):
                        b = 2 + tt4 % 2
                        s_.wait_ge(self.pm.h, pm_base + 12 + tt4 + 1)
                        inst = s_.copy(
                            self.BA[:, (slab * 4 + tt4) * 32:(slab * 4 + tt4 + 1) * 32],
                            bk.f32(b, 32))
                        bk.done(inst, b)
                self.stg.inc(s_.nop(), 1)
            ctx.block.scalar(_a)
        self.stg_all = self.stg.n
        if g == 0:
            self.ba_done = self.stg.n

    def conv(self, g):
        ctx = self.ctx
        first = True
        self.silu_prev = 0
        for cc in range(8):
            for sl in (3, 2, 1, 0):
                def _v(v, cc=cc, sl=sl, first=first, sp=self.silu_prev):
                    if first:
                        v.wait_ge(self.stg.h, self.stg_all)
                        v.wait_ge(self.sg2.h, self.cw_ok)
                        for c2 in range(8):
                            v.memset(self.ST[:, c2 * 2051:c2 * 2051 + 3], 0.0)
                    else:
                        v.wait_ge(self.fa.h, sp)
                    v.tensor_scalar(self.CV[:, :], self.stv_pad(cc, sl * 512, 512),
                                    self.CW[:, cc * 4:cc * 4 + 1], None, ALU.mult)
                    for j in range(1, 4):
                        v.scalar_tensor_tensor(
                            self.CV[:, :], self.stv_pad(cc, sl * 512 + j, 512),
                            self.CW[:, cc * 4 + j:cc * 4 + j + 1], self.CV[:, :],
                            ALU.mult, ALU.add)
                    self.fv.inc(v.engine_nop(), 1)
                ctx.block.vector(_v)
                fvv = self.fv.n
                first = False

                def _a(s_, cc=cc, sl=sl, fvv=fvv):
                    s_.wait_ge(self.fv.h, fvv)
                    self.fa.inc(s_.activation(self.stv(cc, sl * 512, 512),
                                              self.CV[:, :], AF.Silu), 1)
                ctx.block.scalar(_a)
                self.silu_prev = self.fa.n
        self.conv_done = self.fa.n

    def l2norm(self, g):
        ctx, bk = self.ctx, self.ctx.banks
        for cc in range(4):
            isq = cc < 2
            for sl in range(4):
                def _a1(s_, cc=cc, sl=sl):
                    if cc == 0 and sl == 0:
                        s_.wait_ge(self.fa.h, self.conv_done)
                    self.fa.inc(s_.activation(self.QSQ[:, :],
                                              self.stv(cc, sl * 512, 512),
                                              AF.Square), 1)
                ctx.block.scalar(_a1)
                fav = self.fa.n

                def _p1(tt, fav=fav):
                    tt.wait_ge(self.fa.h, fav)
                    tt.wait_ge(self.fv.h, self.onesr_ok)
                    bk.acquire(tt, 6)
                    self.pm.inc(tt.matmul(bk.f32(6)[0:1, :], self.ONESB[:, 0:1],
                                          self.QSQ[:, :], start=True, stop=True), 1)
                ctx.block.tensor(_p1)
                pmv = self.pm.n

                def _v1(v, isq=isq, pmv=pmv):
                    v.wait_ge(self.pm.h, pmv)
                    c2 = 128.0 if isq else 1.0
                    inst = v.tensor_scalar(self.ROW[0:1, :], bk.f32(6)[0:1, :],
                                           c2, c2 * 1e-6, ALU.mult, ALU.add)
                    bk.done(inst, 6)
                ctx.block.vector(_v1)
                b6v = bk.sem[6].n

                def _a2(s_, b6v=b6v):
                    s_.wait_ge(bk.sem[6].h, b6v)
                    ctx.fence(s_, s_.activation(self.ROW[0:1, :], self.ROW[0:1, :],
                                                AF.Ln), "fa")
                    self.fa.inc(s_.activation(self.RWB[0:1, :], self.ROW[0:1, :],
                                              AF.Exp, scale=-0.5), 1)
                ctx.block.scalar(_a2)
                fav3 = self.fa.n

                def _p2(tt, fav3=fav3):
                    tt.wait_ge(self.fa.h, fav3)
                    bk.acquire(tt, 7)
                    self.pm.inc(tt.matmul(bk.f32(7), self.ONESB[0:1, :],
                                          self.RWB[0:1, :], start=True, stop=True), 1)
                ctx.block.tensor(_p2)
                pmv2 = self.pm.n

                def _v2(v, cc=cc, sl=sl, pmv2=pmv2):
                    v.wait_ge(self.pm.h, pmv2)
                    dst = self.stv(cc, sl * 512, 512)
                    inst = v.tensor_tensor(dst, bk.f32(7), dst, ALU.mult)
                    bk.done(inst, 7)
                    self.vm.inc(v.engine_nop(), 1)
                ctx.block.vector(_v2)
        self.l2_done = self.vm.n
    def scalars(self, g):
        ctx, bk = self.ctx, self.ctx.banks
        CL = self.CL
        BET, GCOL, CUMG, NCG, ECU, BTE, EGM, SCR = (CL[0], CL[1], CL[2], CL[3],
                                                    CL[4], CL[5], CL[6], CL[7])
        bav = self.BA[:, :].rearrange("p (t k r) -> p t k r", t=16, k=8)
        bv = bav[:, :, 2 * g:2 * g + 2, 0:2]
        av = bav[:, :, 2 * g:2 * g + 2, 2:4]
        c4 = lambda T: T[:, 0:64].rearrange("p (t k r) -> p t k r", t=16, k=2)

        def _v(v, g=g):
            if g == 0:
                v.wait_ge(self.stg.h, self.ba_done)
                v.wait_ge(self.vm.h, self.nea_ok)
            v.tensor_scalar(c4(SCR), bv, 0.0, None, ALU.min)
            self.fv.inc(v.engine_nop(), 1)
        ctx.block.vector(_v)
        f1 = self.fv.n

        def _a(s_):
            s_.wait_ge(self.fv.h, f1)
            ctx.fence(s_, s_.activation(c4(GCOL), bv, AF.Abs), "fa")
            ctx.fence(s_, s_.activation(c4(GCOL), c4(GCOL), AF.Exp, scale=-1.0), "fa")
            ctx.fence(s_, s_.activation(c4(GCOL), c4(GCOL), AF.Ln, bias=1.0), "fa")
            self.fa.inc(s_.activation(c4(BET), bv, AF.Sigmoid), 1)
        ctx.block.scalar(_a)
        f2 = self.fa.n

        def _v2(v, g=g):
            v.wait_ge(self.fa.h, f2)
            ctx.fence(v, v.tensor_tensor(c4(EGM), c4(SCR), c4(GCOL), ALU.subtract),
                      "fv")
            for t in range(16):
                v.tensor_tensor(SCR[:, t * 4:(t + 1) * 4], av[:, t, :, :],
                                self.DTB[:, 4 * g:4 * g + 4], ALU.add)
            self.fv.inc(v.engine_nop(), 1)
        ctx.block.vector(_v2)
        f3 = self.fv.n

        def _a2(s_):
            s_.wait_ge(self.fv.h, f3)
            ctx.fence(s_, s_.activation(c4(GCOL), c4(SCR), AF.Abs), "fa")
            ctx.fence(s_, s_.activation(c4(GCOL), c4(GCOL), AF.Exp, scale=-1.0), "fa")
            self.fa.inc(s_.activation(c4(GCOL), c4(GCOL), AF.Ln, bias=1.0), 1)
        ctx.block.scalar(_a2)
        f4 = self.fa.n

        def _v3(v, g=g):
            v.wait_ge(self.fa.h, f4)
            ctx.fence(v, v.tensor_scalar(c4(SCR), c4(SCR), 0.0, None, ALU.max), "fv")
            ctx.fence(v, v.tensor_tensor(c4(GCOL), c4(SCR), c4(GCOL), ALU.add), "fv")
            for t in range(16):
                v.tensor_tensor(GCOL[:, t * 4:(t + 1) * 4],
                                GCOL[:, t * 4:(t + 1) * 4],
                                self.NEA[:, 4 * g:4 * g + 4], ALU.mult)
            self.fv.inc(v.engine_nop(), 1)
        ctx.block.vector(_v3)
        f5 = self.fv.n

        def _p(tt):
            tt.wait_ge(self.fv.h, f5)
            for t in range(16):
                b = 4 + (t % 2)
                bk.acquire(tt, b)
                self.pm.inc(tt.matmul(bk.f32(b, 4), self.CTRIU[:, :],
                                      GCOL[:, t * 4:(t + 1) * 4],
                                      start=True, stop=True), 1)
        ctx.block.tensor(_p)
        pv = self.pm.n

        def _a3(s_):
            for t in range(16):
                b = 4 + (t % 2)
                s_.wait_ge(self.pm.h, pv - 16 + t + 1)
                inst = s_.copy(CUMG[:, t * 4:(t + 1) * 4], bk.f32(b, 4))
                bk.done(inst, b)
            self.fa.inc(s_.nop(), 1)
        ctx.block.scalar(_a3)
        f6 = self.fa.n

        def _v3b(v):
            v.wait_ge(self.fa.h, f6)
            self.fv.inc(v.tensor_tensor(self.CPL[:, 0:64], CUMG[:, 0:64],
                                        EGM[:, 0:64], ALU.add), 1)
        ctx.block.vector(_v3b)
        f6b = self.fv.n

        def _p2(tt):
            tt.wait_ge(self.fv.h, f6b)
            for t in range(16):
                b = 4 + (t % 2)
                bk.acquire(tt, b)
                self.pm.inc(tt.matmul(bk.f32(b, 4), self.CEND[:, :],
                                      CUMG[:, t * 4:(t + 1) * 4],
                                      start=True, stop=True), 1)
        ctx.block.tensor(_p2)
        pv2 = self.pm.n

        def _v4(v):
            for t in range(16):
                b = 4 + (t % 2)
                v.wait_ge(self.pm.h, pv2 - 16 + t + 1)
                inst = v.tensor_tensor(NCG[:, t * 4:(t + 1) * 4], bk.f32(b, 4),
                                       CUMG[:, t * 4:(t + 1) * 4], ALU.subtract)
                bk.done(inst, b)
            self.fv.inc(v.engine_nop(), 1)
        ctx.block.vector(_v4)
        f7 = self.fv.n

        def _a4(s_):
            s_.wait_ge(self.fv.h, f7)
            ctx.fence(s_, s_.activation(EGM[:, 0:64], NCG[:, 0:64], AF.Exp), "fa")
            self.fa.inc(s_.activation(ECU[:, 0:64], CUMG[:, 0:64], AF.Exp), 1)
        ctx.block.scalar(_a4)
        f8 = self.fa.n

        def _v5(v):
            v.wait_ge(self.fa.h, f8)
            v.tensor_tensor(BTE[:, 0:64], BET[:, 0:64], ECU[:, 0:64], ALU.mult)
            v.tensor_scalar(NCG[:, 0:64], CUMG[:, 0:64], -1.0, None, ALU.mult)
            self.fv.inc(v.engine_nop(), 1)
        ctx.block.vector(_v5)
        self.cols_done = self.fv.n

        tr_done = [0] * 16
        for t in range(16):
            off = (t % 2) * 8

            def _v7(v, t=t, off=off):
                if t == 0:
                    v.wait_ge(self.fv.h, self.cols_done)
                    dd = getattr(self, "dsc_done", 0)
                    if dd:
                        v.wait_ge(self.sdx.h, dd)
                if t >= 2:
                    v.wait_ge(self.pm.h, tr_done[t - 2])
                tv = self.TSTG[:, off:off + 8].rearrange("p (v k) -> p k v", k=2)
                v.tensor_copy(tv[:, 0, :], self.CPL[:, t * 4:(t + 1) * 4])
                self.fv.inc(v.tensor_copy(tv[:, 1, :],
                                          CUMG[:, t * 4:(t + 1) * 4]), 1)
            ctx.block.vector(_v7)
            fvv = self.fv.n

            def _p3(tt, fvv=fvv, off=off):
                tt.wait_ge(self.fv.h, fvv)
                bk.acquire(tt, 6)
                self.pm.inc(tt.transpose(bk.t[6][0:8, 0:128],
                                         self.TSTG[:, off:off + 8],
                                         self.IDF[:, :]), 1)
            ctx.block.tensor(_p3)
            pvv = self.pm.n
            tr_done[t] = pvv

            def _a5(s_, t=t, pvv=pvv):
                s_.wait_ge(self.pm.h, pvv)
                inst = s_.copy(self.RTT[0:8, t * 128:(t + 1) * 128],
                               bk.t[6][0:8, 0:128])
                bk.done(inst, 6)
                self.fa.inc(s_.nop(), 1)
            ctx.block.scalar(_a5)
        rows_ok = self.fa.n

        def _sr(sync, g=g):
            sync.wait_ge(self.fa.h, rows_ok)
            for vg in range(4):
                srcap = self.RTT[:, :].rearrange("p (n i) -> p n i", n=NCH)[
                    2 * vg:2 * vg + 2, :, :]
                self.sdx.inc(sync.dma_start(
                    out=self.d_sc[4 * g + vg, :, :, :].transpose([1, 0, 2]),
                    in_=srcap), 16)
        ctx.block.sync(_sr)
        self.dsc_done = self.sdx.n

        def _se(sync):
            sync.wait_ge(self.fa.h, f8)
            for par in range(2):
                for vg in range(4):
                    src = ECU[63 + 64 * par: 64 + 64 * par, 0:64]\
                        .rearrange("p (t v) -> p v t", t=16)[:, vg, :]
                    dst = self.EGLROW[0:1, vg * 32:(vg + 1) * 32]\
                        .rearrange("p (t q) -> p q t", q=2)[:, par, :]
                    self.sdx.inc(sync.dma_start(out=dst, in_=src,
                                                allow_slow_non_contiguous=True), 16)
        ctx.block.sync(_se)
        eglr_ok = self.sdx.n

        def _pe(tt):
            tt.wait_ge(self.sdx.h, eglr_ok)
            bk.acquire(tt, 6)
            self.pm.inc(tt.matmul(bk.f32(6, 128), self.ONESR[0:1, :],
                                  self.EGLROW[0:1, :], start=True, stop=True), 1)
        ctx.block.tensor(_pe)
        pv3 = self.pm.n

        def _a6(s_):
            s_.wait_ge(self.pm.h, pv3)
            zg = getattr(self, "zt_guard", 0)
            if zg:
                s_.wait_ge(self.vm.h, zg)
            inst = s_.copy(self.EGLB[:, :], bk.f32(6, 128))
            bk.done(inst, 6)
            self.fa.inc(s_.nop(), 1)
        ctx.block.scalar(_a6)
        self.eglb_done = self.fa.n

    def ebuild(self, g):
        ctx = self.ctx
        NCGC = self.CL[3]
        eprev = getattr(self, "e_done", 0)
        for vg in range(4):
            for half in range(2):
                n0 = half * 16

                def _sd(sync, vg=vg, half=half, n0=n0, eprev=eprev,
                        dtx_free=self.fa.n):
                    if vg == 0 and half == 0:
                        sync.wait_ge(self.sdx.h, self.dsc_done)
                        if eprev:
                            sync.wait_ge(self.fa.h, eprev)
                        g2 = getattr(self, "e_guard", 0)
                        if g2:
                            sync.wait_ge(self.vm.h, g2)
                    else:
                        sync.wait_ge(self.fa.h, dtx_free)
                    for par in range(2):
                        src = bass.AP(
                            self.d_sc.ap().tensor,
                            ((4 * g + vg) * NCH + (n0 + par)) * 128,
                            [[0, 64], [256, 8], [1, 128]])
                        dst = self.DTX[par * 64:(par + 1) * 64, :]\
                            .rearrange("p (n x) -> p n x", n=8)
                        self.sdx.inc(sync.dma_start(out=dst, in_=src), 16)
                ctx.block.sync(_sd)
                sdv = self.sdx.n

                def _vd(v, vg=vg, n0=n0, sdv=sdv):
                    v.wait_ge(self.sdx.h, sdv)
                    if n0 == 0 and vg == 0:
                        v.wait_ge(self.fv.h, self.cols_done)
                    for k in range(16):
                        n = n0 + k
                        par = (n % 2) * 64
                        t = n // 2
                        blk = k // 2
                        dsl = self.DTX[par:par + 64, blk * 128:blk * 128 + 128]
                        v.scalar_tensor_tensor(
                            dsl, dsl,
                            NCGC[par:par + 64, t * 4 + vg:t * 4 + vg + 1],
                            self.CMASK[par:par + 64, :], ALU.add, ALU.add)
                    self.fv.inc(v.engine_nop(), 1)
                ctx.block.vector(_vd)
                fvv = self.fv.n

                def _ae(s_, vg=vg, half=half, fvv=fvv):
                    s_.wait_ge(self.fv.h, fvv)
                    self.fa.inc(s_.activation(
                        self.EBX[vg][:, half * 1024:(half + 1) * 1024],
                        self.DTX[:, :], AF.Exp), 1)
                ctx.block.scalar(_ae)
        self.e_done = self.fa.n


    def tsl(self, vg, slot, par):
        return self.TCH[vg][par:par + 64, slot * 64:(slot + 1) * 64]

    def chunk_loop(self, g):
        ctx, bk = self.ctx, self.ctx.banks
        u = {}
        sq_st = {}
        def _vz(v):
            v.wait_ge(self.vm.h, self.l2_done)
            v.wait_ge(self.fa.h, self.e_done)
            v.wait_ge(self.fa.h, self.eglb_done)
            inst = None
            for vg in range(4):
                v.memset(self.SST[vg][:, :], 0.0)
                inst = v.memset(self.SBF[vg][:, :], 0.0)
            self.vm.inc(inst, 1)
        ctx.block.vector(_vz)
        for vg in range(4):
            u[("sbf", vg)] = (self.vm, self.vm.n)

        bko = lambda vg, i: 2 * vg + (i % 2)

        for n in range(NCH):
            par = (n % 2) * 64
            t = n // 2
            idp = self.ctx.identb[par:par + 64, par:par + 64]

            if n % 2 == 0:
                def _pt(tt, t=t):
                    for khg in range(2):
                        b = bko(2 * khg, 0)
                        bk.acquire(tt, b)
                        self.pm.inc(tt.transpose(
                            bk.bf16(b, 128), self.stv(2 + khg, t * 128, 128),
                            self.ctx.identb[:, :]), 1)
                    for vg in range(4):
                        b = bko(vg, 1)
                        bk.acquire(tt, b)
                        self.pm.inc(tt.transpose(
                            bk.bf16(b, 128), self.stv(4 + vg, t * 128, 128),
                            self.ctx.identb[:, :]), 1)
                ctx.block.tensor(_pt)
                pv = self.pm.n

                def _vt(v, t=t, pv=pv):
                    for khg in range(2):
                        b = bko(2 * khg, 0)
                        v.wait_ge(self.pm.h, pv - 6 + khg + 1)
                        for r in range(2):
                            vg = 2 * khg + r
                            c = t * 4 + vg
                            v.tensor_scalar(self.KBE[vg][:, :], bk.bf16(b, 128),
                                            self.CL[5][:, c:c + 1], None, ALU.mult)
                            inst = v.tensor_scalar(self.KGL[vg][:, :], bk.bf16(b, 128),
                                                   self.CL[6][:, c:c + 1], None,
                                                   ALU.mult)
                            if r == 1:
                                bk.done(inst, b)
                    for vg in range(4):
                        b = bko(vg, 1)
                        v.wait_ge(self.pm.h, pv - 4 + vg + 1)
                        c = t * 4 + vg
                        inst = v.tensor_scalar(self.VBT[vg][:, :], bk.bf16(b, 128),
                                               self.CL[0][:, c:c + 1], None, ALU.mult)
                        bk.done(inst, b)
                    self.vm.inc(v.engine_nop(), 1)
                ctx.block.vector(_vt)
                tile_ok = self.vm.n
                for vg in range(4):
                    u[("tile", vg)] = tile_ok

            def _pa(tt, n=n, par=par, t=t):
                for vg in range(4):
                    khg = vg // 2
                    bq = bko(vg, 0)
                    bkk = bko(vg, 1)
                    if ("qet_done", vg) in u:
                        tt.wait_ge(self.am.h, u[("qet_done", vg)])
                    bk.acquire(tt, bq)
                    self.pm.inc(tt.transpose(
                        bk.bf16(bq, 128)[par:par + 64, :],
                        self.stv(khg, n * 64, 64),
                        self.ctx.identb[:, :], tile_position=(0, par)), 1)
                    u[("psq", vg)] = (bq, self.pm.n)
                    if vg % 2 == 0:
                        bk.acquire(tt, bkk)
                        tt.matmul(bk.f32(bkk, 128)[par:par + 64, 0:64],
                                  self.stv(2 + khg, n * 64, 64),
                                  self.stv(2 + khg, n * 64, 64),
                                  start=True, stop=True, tile_position=(0, par))
                        self.pm.inc(tt.matmul(
                            bk.f32(bkk, 128)[par:par + 64, 64:128],
                            self.stv(2 + khg, n * 64, 64),
                            self.stv(khg, n * 64, 64),
                            start=True, stop=True, tile_position=(0, par)), 1)
                        u[("pskk", khg)] = (bkk, self.pm.n)
            ctx.block.tensor(_pa)

            def _vb(v, n=n, par=par, t=t):
                for vg in range(4):
                    bq, pv = u[("psq", vg)]
                    v.wait_ge(self.pm.h, pv)
                    c = t * 4 + vg
                    inst = v.tensor_scalar(
                        self.QET[vg][par:par + 64, :],
                        bk.bf16(bq, 128)[par:par + 64, :],
                        self.CL[4][par:par + 64, c:c + 1], None, ALU.mult)
                    bk.done(inst, bq)
                for vg in range(4):
                    khg = vg // 2
                    bkk, pv = u[("pskk", khg)]
                    v.wait_ge(self.pm.h, pv)
                    es_ = self.eslice(vg, n)
                    v.tensor_tensor(self.tsl(vg, 0, par),
                                    bk.t[bkk][par:par + 64, 0:64], es_[:, 0:64],
                                    ALU.mult)
                    inst = v.tensor_tensor(self.ATN[vg][par:par + 64, :],
                                           bk.t[bkk][par:par + 64, 64:128],
                                           es_[:, 64:128], ALU.mult)
                    if vg % 2 == 1:
                        bk.done(inst, bkk)
                self.vm.inc(v.engine_nop(), 1)
            ctx.block.vector(_vb)
            vb_ok = self.vm.n

            def _pc(tt, n=n, par=par, vb_ok=vb_ok):
                tt.wait_ge(self.vm.h, vb_ok)
                idp = self.ctx.identb[par:par + 64, par:par + 64]
                for vg in range(4):
                    bq = bko(vg, 0)
                    bl = bko(vg, 1)
                    bk.acquire(tt, bq)
                    self.pm.inc(tt.transpose(
                        bk.bf16(bq, 64), self.QET[vg][par:par + 64, :],
                        idp, tile_position=(par, 0)), 1)
                    u[("psq2", vg)] = (bq, self.pm.n)
                    bk.acquire(tt, bl)
                    self.pm.inc(tt.transpose(
                        bk.bf16(bl, 64)[par:par + 64, 0:64],
                        self.tsl(vg, 0, par), idp,
                        tile_position=(par, par)), 1)
                    u[("psl", vg)] = (bl, self.pm.n)
            ctx.block.tensor(_pc)

            def _ad(s_, n=n, par=par):
                for vg in range(4):
                    bq, pv = u[("psq2", vg)]
                    s_.wait_ge(self.pm.h, pv)
                    inst = s_.copy(self.QEF[vg][:, :], bk.bf16(bq, 64))
                    bk.done(inst, bq)
                    bl, pv2 = u[("psl", vg)]
                    s_.wait_ge(self.pm.h, pv2)
                    inst = s_.copy(self.tsl(vg, 1, par),
                                   bk.bf16(bl, 64)[par:par + 64, 0:64])
                    bk.done(inst, bl)
                    u[("add", vg)] = (bk.sem[bl], bk.sem[bl].n)
                    u[("qefd", vg)] = (bk.sem[bq], bk.sem[bq].n)
                self.am.inc(s_.nop(), 1)
                for vg in range(4):
                    u[("qet_done", vg)] = self.am.n
            ctx.block.scalar(_ad)
            ad_ok = self.am.n

            def _vp0(v, n=n, par=par, ad_ok=ad_ok):
                v.wait_ge(self.am.h, ad_ok)
                for vg in range(4):
                    v.tensor_tensor(self.tsl(vg, 6, par),
                                    self.ctx.identb[par:par + 64, par:par + 64],
                                    self.tsl(vg, 0, par), ALU.subtract)
                self.vm.inc(v.engine_nop(), 1)
            ctx.block.vector(_vp0)
            p_ok = self.vm.n

            rounds = [(0, 1, 2, 3), (3, 2, 4, 5), (5, 4, 2, 3), (3, 2, 4, 5),
                      (5, 4, 2, None)]
            prev_ok = ad_ok
            for ri, (sa, sb_, so1, so2) in enumerate(rounds):
                def _pe(tt, n=n, par=par, sa=sa, sb_=sb_, so2=so2,
                        prev_ok=prev_ok, ri=ri):
                    tt.wait_ge(self.am.h, prev_ok)
                    for vg in range(4):
                        key = ("add", vg) if ri == 0 else ("aed", vg)
                        if key in u:
                            ws_, wv_ = u[key]
                            tt.wait_ge(ws_.h, wv_)
                        b = bko(vg, 0)
                        bk.acquire(tt, b)
                        self.pm.inc(tt.matmul(
                            bk.t[b][par:par + 64, 0:64],
                            self.tsl(vg, sa, par), self.tsl(vg, sb_, par),
                            start=True, stop=True, tile_position=(par, par)), 1)
                        if so2 is not None:
                            self.pm.inc(tt.matmul(
                                bk.t[b][par:par + 64, 64:128],
                                self.tsl(vg, sb_, par), self.tsl(vg, sa, par),
                                start=True, stop=True,
                                tile_position=(par, par)), 1)
                        u[("sq", vg)] = (b, self.pm.n)
                ctx.block.tensor(_pe)

                def _ae(s_, n=n, par=par, so1=so1, so2=so2):
                    for vg in range(4):
                        b, pv = u[("sq", vg)]
                        s_.wait_ge(self.pm.h, pv)
                        inst = s_.copy(self.tsl(vg, so1, par),
                                       bk.t[b][par:par + 64, 0:64])
                        if so2 is not None:
                            inst = s_.copy(self.tsl(vg, so2, par),
                                           bk.t[b][par:par + 64, 64:128])
                        bk.done(inst, b)
                        u[("aed", vg)] = (bk.sem[b], bk.sem[b].n)
                    self.am.inc(s_.nop(), 1)
                ctx.block.scalar(_ae)
                prev_ok = self.am.n

                def _pf(tt, n=n, par=par, so1=so1, p_ok=p_ok, prev_ok=prev_ok):
                    tt.wait_ge(self.am.h, prev_ok)
                    tt.wait_ge(self.vm.h, p_ok)
                    for vg in range(4):
                        ws_, wv_ = u[("aed", vg)]
                        tt.wait_ge(ws_.h, wv_)
                        b = bko(vg, 1)
                        bk.acquire(tt, b)
                        self.pm.inc(tt.matmul(
                            bk.t[b][par:par + 64, 0:64],
                            self.tsl(vg, so1, par), self.tsl(vg, 6, par),
                            start=True, stop=True, tile_position=(par, par)), 1)
                        u[("pp", vg)] = (b, self.pm.n)
                ctx.block.tensor(_pf)

                def _vg2(v, n=n, par=par):
                    for vg in range(4):
                        b, pv = u[("pp", vg)]
                        v.wait_ge(self.pm.h, pv)
                        inst = v.tensor_tensor(self.tsl(vg, 6, par),
                                               bk.t[b][par:par + 64, 0:64],
                                               self.tsl(vg, 6, par), ALU.add)
                        bk.done(inst, b)
                    self.vm.inc(v.engine_nop(), 1)
                ctx.block.vector(_vg2)
                p_ok = self.vm.n

            def _vh(v, n=n, par=par, p_ok=p_ok):
                v.wait_ge(self.vm.h, p_ok)
                for vg in range(4):
                    v.tensor_scalar(self.NTV[vg][par:par + 64, :],
                                    self.tsl(vg, 6, par), -1.0, None, ALU.mult)
                self.vm.inc(v.engine_nop(), 1)
            ctx.block.vector(_vh)
            ntt_ok = self.vm.n

            def _ph(tt, n=n, par=par, ntt_ok=ntt_ok):
                tt.wait_ge(self.vm.h, ntt_ok)
                for vg in range(4):
                    b = bko(vg, 0)
                    bk.acquire(tt, b)
                    self.pm.inc(tt.matmul(
                        bk.f32(b, 64), self.KBE[vg][par:par + 64, :],
                        self.NTV[vg][par:par + 64, :],
                        start=True, stop=True, tile_position=(par, 0)), 1)
                    u[("kcd", vg)] = (b, self.pm.n)
            ctx.block.tensor(_ph)

            def _ah(s_, n=n):
                for vg in range(4):
                    b, pv = u[("kcd", vg)]
                    s_.wait_ge(self.pm.h, pv)
                    inst = s_.copy(self.KCDN[vg][:, :], bk.f32(b, 64))
                    bk.done(inst, b)
                    u[("kcdd", vg)] = (bk.sem[b], bk.sem[b].n)
                self.am.inc(s_.nop(), 1)
            ctx.block.scalar(_ah)
            kcd_ok = self.am.n

            def _pi(tt, n=n, par=par, kcd_ok=kcd_ok):
                tt.wait_ge(self.am.h, kcd_ok)
                if n == 0:
                    return
                for vg in range(4):
                    b = bko(vg, 0)
                    sb_ = u[("sbf", vg)]
                    tt.wait_ge(sb_[0].h, sb_[1])
                    ks_, kv_ = u[("kcdd", vg)]
                    tt.wait_ge(ks_.h, kv_)
                    bk.acquire(tt, b)
                    self.pm.inc(tt.matmul(
                        bk.f32(b, 128)[par:par + 64, :], self.KCDN[vg][:, :],
                        self.SBF[vg][:, :],
                        start=True, stop=True, tile_position=(0, par)), 1)
                    u[("vpr", vg)] = (b, self.pm.n)
            ctx.block.tensor(_pi)

            def _ai(s_, n=n, par=par):
                if n == 0:
                    self.am.inc(s_.nop(), 1)
                    return
                for vg in range(4):
                    b, pv = u[("vpr", vg)]
                    s_.wait_ge(self.pm.h, pv)
                    inst = s_.copy(self.VPRN[vg][par:par + 64, :],
                                   bk.f32(b, 128)[par:par + 64, :])
                    bk.done(inst, b)
                    u[("vprd", vg)] = (bk.sem[b], bk.sem[b].n)
                self.am.inc(s_.nop(), 1)
            ctx.block.scalar(_ai)
            vpr_ok = self.am.n

            def _pj(tt, n=n, par=par, vpr_ok=vpr_ok):
                tt.wait_ge(self.am.h, vpr_ok)
                for vg in range(4):
                    b = bko(vg, 1)
                    tt.wait_ge(self.vm.h, u[("tile", vg)])
                    bk.acquire(tt, b)
                    self.pm.inc(tt.matmul(
                        bk.f32(b, 128)[par:par + 64, :], self.tsl(vg, 6, par),
                        self.VBT[vg][par:par + 64, :],
                        start=True, stop=True, tile_position=(par, par)), 1)
                    u[("vnw", vg)] = (b, self.pm.n)
            ctx.block.tensor(_pj)

            def _vj(v, n=n, par=par):
                for vg in range(4):
                    b, pv = u[("vnw", vg)]
                    v.wait_ge(self.pm.h, pv)
                    if n > 0:
                        ps_, pv_ = u[("vprd", vg)]
                        v.wait_ge(ps_.h, pv_)
                    if n == 0:
                        inst = v.tensor_copy(self.VNB[vg][par:par + 64, :],
                                             bk.t[b][par:par + 64, 0:128])
                    else:
                        inst = v.tensor_tensor(self.VNB[vg][par:par + 64, :],
                                               bk.t[b][par:par + 64, 0:128],
                                               self.VPRN[vg][par:par + 64, :],
                                               ALU.add)
                    bk.done(inst, b)
                    u[("vnbd", vg)] = (bk.sem[b], bk.sem[b].n)
                self.vm.inc(v.engine_nop(), 1)
            ctx.block.vector(_vj)
            vn_ok = self.vm.n

            def _pk(tt, n=n, par=par):
                if n == 0:
                    return
                for vg in range(4):
                    qs_, qv_ = u[("qefd", vg)]
                    tt.wait_ge(qs_.h, qv_)
                    b = bko(vg, 0)
                    bk.acquire(tt, b)
                    self.pm.inc(tt.matmul(
                        bk.f32(b, 128)[par:par + 64, :], self.QEF[vg][:, :],
                        self.SBF[vg][:, :],
                        start=True, stop=True, tile_position=(0, par)), 1)
                    u[("int", vg)] = (b, self.pm.n)
            ctx.block.tensor(_pk)

            def _ak(s_, n=n, par=par):
                if n == 0:
                    self.am.inc(s_.nop(), 1)
                    return
                for vg in range(4):
                    b, pv = u[("int", vg)]
                    s_.wait_ge(self.pm.h, pv)
                    inst = s_.copy(self.SIX[vg][par:par + 64, :],
                                   bk.f32(b, 128)[par:par + 64, :])
                    bk.done(inst, b)
                    u[("sixd", vg)] = (bk.sem[b], bk.sem[b].n)
                self.am.inc(s_.nop(), 1)
            ctx.block.scalar(_ak)
            int_ok = self.am.n

            def _pl(tt, n=n, par=par, vn_ok=vn_ok):
                tt.wait_ge(self.vm.h, vn_ok)
                for vg in range(4):
                    vs_, vv_ = u[("vnbd", vg)]
                    tt.wait_ge(vs_.h, vv_)
                    b = bko(vg, 1)
                    bk.acquire(tt, b)
                    self.pm.inc(tt.matmul(
                        bk.f32(b, 128)[par:par + 64, :],
                        self.ATN[vg][par:par + 64, :],
                        self.VNB[vg][par:par + 64, :],
                        start=True, stop=True, tile_position=(par, par)), 1)
                    u[("atp", vg)] = (b, self.pm.n)
            ctx.block.tensor(_pl)

            def _vl(v, n=n, par=par, t=t, int_ok=int_ok):
                v.wait_ge(self.am.h, int_ok)
                for vg in range(4):
                    b, pv = u[("atp", vg)]
                    v.wait_ge(self.pm.h, pv)
                    gt = self.GTX[vg][par:par + 64, :]
                    if n == 0:
                        inst = v.tensor_copy(gt, bk.t[b][par:par + 64, 0:128])
                    else:
                        ss_, sv_ = u[("sixd", vg)]
                        v.wait_ge(ss_.h, sv_)
                        inst = v.tensor_tensor(gt, bk.t[b][par:par + 64, 0:128],
                                               self.SIX[vg][par:par + 64, :],
                                               ALU.add)
                    bk.done(inst, b)
                    v.tensor_tensor(gt, gt,
                                    self.ZT[par:par + 64,
                                            t * 512 + vg * 128: t * 512 + (vg + 1) * 128],
                                    ALU.mult)
                    msc = self.SMA[par:par + 64, vg * 4:vg * 4 + 1]
                    ctx.fence(v, v.scalar_tensor_tensor(
                        self.DTX[par:par + 64, 0:128], gt, 1.0, gt,
                        ALU.bypass, ALU.mult, accum_out=msc), "fv")
                    ctx.fence(v, v.tensor_scalar(msc, msc, 1.0 / 128, 1e-6,
                                                 ALU.mult, ALU.add), "fv")
                self.vm.inc(v.engine_nop(), 1)
            ctx.block.vector(_vl)
            gt_ok = self.vm.n

            def _al(s_, n=n, par=par, gt_ok=gt_ok):
                s_.wait_ge(self.vm.h, gt_ok)
                for vg in range(4):
                    msc = self.SMA[par:par + 64, vg * 4:vg * 4 + 1]
                    ctx.fence(s_, s_.activation(msc, msc, AF.Ln), "fa")
                    s_.activation(msc, msc, AF.Exp, scale=-0.5)
                self.fa.inc(s_.nop(), 1)
            ctx.block.scalar(_al)
            rstd_ok = self.fa.n

            def _vm2(v, n=n, par=par, rstd_ok=rstd_ok):
                v.wait_ge(self.fa.h, rstd_ok)
                for vg in range(4):
                    msc = self.SMA[par:par + 64, vg * 4:vg * 4 + 1]
                    self.vm.inc(v.tensor_scalar(
                        self.XNTM[vg][par:par + 64, :],
                        self.GTX[vg][par:par + 64, :], msc, None,
                        ALU.mult), 1)
            ctx.block.vector(_vm2)
            xn_ok = self.vm.n

            def _pn(tt, n=n, par=par, xn_ok=xn_ok):
                tt.wait_ge(self.vm.h, xn_ok)
                idp = self.ctx.identb[par:par + 64, par:par + 64]
                for vg in range(4):
                    b = bko(vg, 0)
                    bk.acquire(tt, b)
                    self.pm.inc(tt.transpose(
                        bk.bf16(b, 64), self.XNTM[vg][par:par + 64, :], idp,
                        tile_position=(par, 0)), 1)
                    u[("xnt", vg)] = (b, self.pm.n)
            ctx.block.tensor(_pn)

            def _an(s_, g=g, n=n):
                if n == 0:
                    xg = getattr(self, "xnp_sent", 0)
                    if xg:
                        s_.wait_ge(self.sdx.h, xg)
                for vg in range(4):
                    b, pv = u[("xnt", vg)]
                    s_.wait_ge(self.pm.h, pv)
                    dst = self.XNU[:, vg * S + n * 64: vg * S + (n + 1) * 64]
                    inst = s_.copy(dst, bk.bf16(b, 64))
                    bk.done(inst, b)
                self.am.inc(s_.nop(), 1)
            ctx.block.scalar(_an)
            u["xn_last"] = self.am.n

            def _po(tt, n=n, par=par, vn_ok=vn_ok):
                for vg in range(4):
                    vs_, vv_ = u[("vnbd", vg)]
                    tt.wait_ge(vs_.h, vv_)
                    b = bko(vg, 1)
                    bk.acquire(tt, b)
                    self.pm.inc(tt.matmul(
                        bk.f32(b, 128), self.KGL[vg][par:par + 64, :],
                        self.VNB[vg][par:par + 64, :],
                        start=True, stop=True, tile_position=(par, 0)), 1)
                    u[("sup", vg)] = (b, self.pm.n)
            ctx.block.tensor(_po)

            def _vo(v, n=n):
                for vg in range(4):
                    b, pv = u[("sup", vg)]
                    v.wait_ge(self.pm.h, pv)
                    inst = v.scalar_tensor_tensor(
                        self.SST[vg][:, :], self.SST[vg][:, :],
                        self.EGLB[:, vg * 32 + n:vg * 32 + n + 1],
                        bk.t[b][:, 0:128], ALU.mult, ALU.add)
                    bk.done(inst, b)
                    u[("supd", vg)] = (bk.sem[b], bk.sem[b].n)
            ctx.block.vector(_vo)

            def _ao(s_, n=n):
                for vg in range(4):
                    sm_, val = u[("supd", vg)]
                    s_.wait_ge(sm_.h, val)
                    self.am.inc(s_.copy(self.SBF[vg][:, :],
                                        self.SST[vg][:, :]), 1)
                for vg in range(4):
                    u[("sbf", vg)] = (self.am, self.am.n)
            ctx.block.scalar(_ao)
        self.xn_done = u["xn_last"]

    def xnp_send(self, g):
        ctx = self.ctx

        def _gs(gp, g=g):
            gp.wait_ge(self.am.h, self.xn_done)
            self.sg2.inc(gp.dma_start(
                out=self.d_xnu[:, :, :].rearrange("v p t -> p v t"),
                in_=self.XNU[:, :].rearrange("p (v t) -> p v t", v=4)), 16)
            gp.wait_ge(self.sg2.h, self.sg2.n)
            xt = self.d_xnu.ap().tensor
            gp.reg_load(self.rtmp, self.POFS[0:1, 0:1])
            src_own = bass.AP(xt, self.rtmp,
                              [[S, 128], [128 * S, 4], [1, TOWN]])
            self.sg2.inc(gp.dma_start(
                out=self.d_xn[4 * g:4 * g + 4, :, :].rearrange("v p t -> p v t"),
                in_=src_own), 16)
            gp.reg_load(self.rtmp2, self.POFS[0:1, 1:2])
            src_par = bass.AP(xt, self.rtmp2,
                              [[S, 128], [128 * S, 4], [1, TOWN]])
            self.sg2.inc(gp.dma_start(
                out=self.cc_xn_src[4 * g:4 * g + 4, :, :].rearrange("v p t -> p v t"),
                in_=src_par), 16)
        ctx.block.gpsimd(_gs)
        self.xnp_sent = self.sg2.n


    def outproj(self, es_op, resid2):
        ctx, bk = self.ctx, self.ctx.banks
        WOS = ctx.sb("at_wos", 32 * 512, BF16, es_op)
        XPN = ctx.sb("at_xpn", 16 * TOWN, BF16, es_op)
        XNO = ctx.sb("at_xno", 16 * TOWN, BF16, es_op)
        self.XNO = XNO
        NWC = ctx.sb("at_nwc", 1, F32, es_op)

        def _sl(sync):
            sync.wait_ge(self.sg2.h, self.xnp_sent)
            self.sdx.inc(sync.dma_start(
                out=XNO[:, :].rearrange("p (v t) -> p v t", v=16),
                in_=self.d_xn[:, :, :].rearrange("v p t -> p v t")), 16)
        ctx.block.sync(_sl)
        xno_ok = self.sdx.n
        emit_tap_sbuf(ctx, "atxno", XNO[:, :], (128, 16 * TOWN),
                      self.sdx, xno_ok, BF16)

        def _gn(g_):
            ctx.sem("at_wold").inc(g_.dma_start(out=NWC[:, :],
                                                in_=ctx.inp["nw"][:, :]), 16)
        ctx.block.gpsimd(_gn)
        X1T = [ctx.sb(f"at_x1t{i}", 512, F32, es_op) for i in range(2)]

        def _gx(g_):
            g_.wait_ge(self.sdx.h, self.xnp_sent)
            self.scc.inc(g_.collective_compute(
                "AllGather", ALU.bypass,
                ins=[self.cc_xn_src.ap().opt()], outs=[self.cc_xn_dst.ap().opt()],
                replica_groups=[[0, 1], [2, 3], [4, 5], [6, 7]]), 1)
        ctx.block.gpsimd(_gx)
        cc_xn_done = self.scc.n

        def _sx(g_):
            g_.wait_ge(self.scc.h, cc_xn_done)
            base = self.cc_xn_dst[0:16, :, :].rearrange("v p t -> p v t")
            src = bass.AP(base.tensor, self.rxpn, base.ap)
            ctx.sem("at_xpns").inc(g_.dma_start(
                out=XPN[:, :].rearrange("p (v t) -> p v t", v=16),
                in_=src), 16)
        ctx.block.gpsimd(_sx)
        xpn_ok = ctx.sem("at_xpns").n
        self.xpn_ok_ = xpn_ok

        x1used = 0
        for of in range(NOF):
            def _gw(g_, of=of):
                if of > 0:
                    g_.wait_ge(ctx.sem("at_wofr").h, of)
                ctx.sem("at_wold").inc(g_.dma_start(
                    out=WOS[:, :],
                    in_=ctx.inp["wout"][of].rearrange("p v x -> p (v x)")), 16)
            ctx.block.gpsimd(_gw)
            wo_t = ctx.sem("at_wold").n

            def _vn(v, wo_t=wo_t):
                v.wait_ge(ctx.sem("at_wold").h, wo_t)
                self.vm.inc(v.tensor_scalar(WOS[:, :], WOS[:, :], NWC[:, 0:1],
                                            None, ALU.mult), 1)
            ctx.block.vector(_vn)
            wos_ok = self.vm.n

            for t in range(NTT):
                def _s1(sync, of=of, t=t, x1used=x1used):
                    if of * NTT + t >= 2:
                        sync.wait_ge(self.vm.h, x1used)
                    self.sdx.inc(sync.dma_start(
                        out=X1T[t % 2][:, :],
                        in_=self.d_x1[:, t * D + of * 512: t * D + of * 512 + 512]), 16)
                ctx.block.sync(_s1)
                x1_t = self.sdx.n

                def _pw(tt, of=of, t=t, wos_ok=wos_ok, xno_ok=xno_ok):
                    if t == 0:
                        tt.wait_ge(self.vm.h, wos_ok)
                        if of == 0:
                            tt.wait_ge(ctx.sem("at_xpns").h, self.xpn_ok_)
                            tt.wait_ge(self.sdx.h, xno_ok)
                    b = 4 + (t % 4)
                    bk.acquire(tt, b)
                    for vh in range(32):
                        if vh < 16:
                            lhs = XNO[:, vh * TOWN + t * 128:
                                      vh * TOWN + (t + 1) * 128]
                        else:
                            lhs = XPN[:, (vh - 16) * TOWN + t * 128:
                                      (vh - 16) * TOWN + (t + 1) * 128]
                        inst = tt.matmul(bk.f32(b), lhs,
                                         WOS[:, vh * 512:(vh + 1) * 512],
                                         start=(vh == 0), stop=(vh == 31))
                    self.pm.inc(inst, 1)
                ctx.block.tensor(_pw)
                pw_t = self.pm.n

                def _vw(v, of=of, t=t, x1_t=x1_t, pw_t=pw_t):
                    v.wait_ge(self.sdx.h, x1_t)
                    v.wait_ge(self.pm.h, pw_t)
                    b = 4 + (t % 4)
                    inst = v.tensor_tensor(
                        resid2[:, t * D + of * 512: t * D + of * 512 + 512],
                        bk.f32(b), X1T[t % 2][:, :], ALU.add)
                    bk.done(inst, b)
                    self.vm.inc(v.engine_nop(), 1)
                ctx.block.vector(_vw)
                x1used = self.vm.n
            def _vr(v, pw_t=self.pm.n):
                v.wait_ge(self.pm.h, pw_t)
                ctx.sem("at_wofr").inc(v.engine_nop(), 1)
            ctx.block.vector(_vr)
        return self.vm, self.vm.n


def phase_attention(ctx, es_outer, d_x1, cc_xh_src, x1_dumped):
    es_g = ExitStack()
    ab = AttnBuilder(ctx, es_outer, es_g, d_x1, cc_xh_src, x1_dumped)
    with es_g:
        ab.prologue()
        for g in range(NG):
            ab.slab_loop(g)
            ab.conv(g)
            ab.l2norm(g)
            ab.scalars(g)
            ab.ebuild(g)
            if g == 0:
                emit_tap_sbuf(ctx, "atst", ab.ST[:, 0:8 * 2051], (128, 8 * 2051),
                              ab.vm, ab.l2_done, BF16)
                emit_tap_sbuf(ctx, "atzt", ab.ZT[:, :], (128, 16 * 512),
                              ab.stg, ab.stg_all, BF16)
                emit_tap_sbuf(ctx, "atba", ab.BA[:, :], (128, 512),
                              ab.stg, ab.stg_all, F32)
                emit_tap_sbuf(ctx, "atcumg", ab.CL[2][:, :], (128, 64),
                              ab.fv, ab.cols_done, F32)
                emit_tap_sbuf(ctx, "atbet", ab.CL[0][:, :], (128, 64),
                              ab.fv, ab.cols_done, F32)
                emit_tap_sbuf(ctx, "atg", ab.CL[1][:, :], (128, 64),
                              ab.fv, ab.cols_done, F32)
                emit_tap_sbuf(ctx, "ateb", ab.EBX[0][:, :], (128, 16 * 128),
                              ab.fa, ab.e_done, BF16)
                emit_tap_sbuf(ctx, "ategl", ab.EGLB[:, :], (128, 128),
                              ab.fa, ab.eglb_done, F32)
                emit_tap_sbuf(ctx, "atnea", ab.NEA[:, :], (128, 16),
                              ab.fv, ab.cols_done, F32)
                emit_tap_sbuf(ctx, "atdtb", ab.DTB[:, :], (128, 16),
                              ab.fv, ab.cols_done, F32)
                emit_tap_sbuf(ctx, "atscr", ab.CL[7][:, :], (128, 64),
                              ab.fv, ab.cols_done, F32)
            ab.chunk_loop(g)
            if g == 0:
                emit_tap_sbuf(ctx, "atxnu", ab.XNU[:, :], (128, 4 * S),
                              ab.am, ab.xn_done, BF16)
                for _vg in (0, 2):
                    for _nm, _buf, _w, _dt in (
                            ("eb", ab.EBX, 16 * 128, BF16),
                            ("kbe", ab.KBE, 128, BF16),
                            ("kgl", ab.KGL, 128, BF16),
                            ("vbt", ab.VBT, 128, BF16),
                            ("qef", ab.QEF, 64, BF16),
                            ("atn", ab.ATN, 64, BF16),
                            ("tch", ab.TCH, 7 * 64, BF16),
                            ("kcdn", ab.KCDN, 64, BF16),
                            ("vnb", ab.VNB, 128, BF16),
                            ("six", ab.SIX, 128, F32),
                            ("gtx", ab.GTX, 128, F32),
                            ("sbf", ab.SBF, 128, BF16)):
                        emit_tap_sbuf(ctx, f"atv{_vg}{_nm}",
                                      _buf[_vg][:, :], (128, _w),
                                      ab.am, ab.xn_done, _dt)
            ab.zt_guard = ab.vm.n
            ab.e_guard = ab.vm.n
            ab.xnp_send(g)
    return ab



def emit_tap_sbuf(ctx, name, src_ap, shape, ready_sem, ready_val, dtype=F32):
    if name not in ctx.taps:
        return
    o = ctx.nc.dram_tensor(f"tap_{name}", list(shape), dtype, kind="ExternalOutput")
    ctx.tap_shapes[name] = shape

    def _(sync):
        sync.wait_ge(ready_sem.h, ready_val)
        ctx.sem("st").inc(sync.dma_start(out=o[:, :], in_=src_ap), 16)
    ctx.block.sync(_)



def build_nc(taps=()):
    nc = bass.Bass("TRN2", target_bir_lowering=False, detect_race_conditions=False)

    inp = {}
    def din(name, shape):
        inp[name] = nc.dram_tensor(name, list(shape), F32, kind="ExternalInput")
        return inp[name]

    din("x0", [TOWN, D])
    din("wr", [128, FC, 8])
    din("wsg", [SF, 128, FC, 128])
    din("wsu", [SF, 128, FC, 128])
    din("wsd", [NOF, 128, SF, 512])
    din("weg", [NE, EF, 128, FC, 128])
    din("weu", [NE, EF, 128, FC, 128])
    din("wed", [NE, NOF, 128, EF, 512])
    din("cident", [128, 128])
    din("wqkv", [NG, 128, 8 * FC, 128])
    din("wz", [NG, 128, FC, 512])
    din("wba", [128, FC, 32])
    din("wconv", [NG, 128, 8, 4])
    din("alog", [1, 16])
    din("dtb", [1, 16])
    din("wout", [NOF, 128, 32, 512])
    din("cmask", [128, 128])
    din("ctriu", [128, 128])
    din("cend", [128, 128])
    din("nw", [128, 1])
    inp["pofs"] = nc.dram_tensor("pofs", [1, 12], mybir.dt.int32,
                                 kind="ExternalInput")
    out = nc.dram_tensor("out", [TOWN, D], F32, kind="ExternalOutput")

    es = ExitStack()
    with es:
        block = es.enter_context(nc.Block())
        ctx = Ctx(nc, es, block, taps)
        ctx.inp = inp
        ctx.banks = Banks(ctx)
        ctx.identb = ctx.sb("identb", [128, 128], BF16)
        ctx.wrs = ctx.sb("wrs", FC * 8, BF16)
        d_x1 = nc.dram_tensor("d_x1", [128, NTT * D], F32)

        es1 = ExitStack()
        resid = ctx.sb("resid", NTT * D, F32, es1)
        xh = ctx.sb("xh", FC * TOWN, BF16, es1)

        ldx = ctx.sem("ldx")
        ldg = ctx.sem("ldg")

        def _s(sync):
            ldx.inc(sync.dma_start(
                out=resid[:, :].rearrange("p (t d) -> p t d", t=NTT),
                in_=inp["x0"].ap().rearrange("(t p) d -> p t d", p=128)), 16)
        ctx.block.sync(_s)

        def _g(g):
            ldg.inc(g.dma_start(out=ctx.identb[:, :], in_=inp["cident"][:, :]), 16)
        ctx.block.gpsimd(_g)

        def _v(v):
            v.wait_ge(ldx.h, 16)
        ctx.block.vector(_v)

        def _p(tt):
            tt.wait_ge(ldg.h, 16)
        ctx.block.tensor(_p)

        es_ln1 = ExitStack()
        with es_ln1:
            ln1, ln1v = phase_layernorm(ctx, resid, xh, "ln1", es_ln1)
        emit_tap_sbuf(ctx, "xh1", xh[:, :], (128, FC * TOWN), ln1, ln1v, BF16)

        phase_ffn(ctx, resid, xh, ln1, ln1v, "f1")
        x1done = ctx.sem("mil")

        def _vm(v):
            x1done.inc(v.engine_nop(), 1)
        ctx.block.vector(_vm)
        emit_tap_sbuf(ctx, "x1", resid[:, :], (128, NTT * D), x1done, x1done.n, F32)

        es_ln2 = ExitStack()
        with es_ln2:
            ln2, ln2v = phase_layernorm(ctx, resid, xh, "ln2", es_ln2)
        emit_tap_sbuf(ctx, "xh2", xh[:, :], (128, FC * TOWN), ln2, ln2v, BF16)

        cc_xh_src = nc.dram_tensor("cc_xh_src", [FC, 128, TOWN], BF16)
        sdx = ctx.sem("at_dx_")

        def _sp(sync):
            sync.wait_ge(x1done.h, x1done.n)
            sdx.inc(sync.dma_start(
                out=d_x1[:, :].rearrange("p (t d) -> p t d", t=NTT),
                in_=resid[:, :].rearrange("p (t d) -> p t d", t=NTT)), 16)
            sync.wait_ge(ln2.h, ln2v)
            sdx.inc(sync.dma_start(
                out=cc_xh_src[:, :, :].rearrange("f p t -> p f t"),
                in_=xh[:, :].rearrange("p (f t) -> p f t", f=FC)), 16)
        ctx.block.sync(_sp)
        x1_dumped = sdx.n
        es1.close()

        es_at = ExitStack()
        ab = phase_attention(ctx, es_at, d_x1, cc_xh_src, x1_dumped)

        es2 = ExitStack()
        resid2 = ctx.sb("resid2", NTT * D, F32, es2)
        xh2 = ctx.sb("xh2", FC * TOWN, BF16, es2)
        es_op = ExitStack()
        with es_op:
            x2s, x2v = ab.outproj(es_op, resid2)
        es_at.close()
        emit_tap_sbuf(ctx, "x2", resid2[:, :], (128, NTT * D), x2s, x2v, F32)

        ctx.ffn_fill_guard = (ctx.sem("at_pm"), ctx.sem("at_pm").n)
        ctx.h_guard = (ctx.sem("at_pm"), ctx.sem("at_pm").n)

        es_ln3 = ExitStack()
        with es_ln3:
            ln3, ln3v = phase_layernorm(ctx, resid2, xh2, "ln3", es_ln3)
        phase_ffn(ctx, resid2, xh2, ln3, ln3v, "f2")
        x3done = ctx.sem("mil")

        def _vm2(v):
            x3done.inc(v.engine_nop(), 1)
        ctx.block.vector(_vm2)

        st = ctx.sem("st")

        def _so(sync):
            sync.wait_ge(x3done.h, x3done.n)
            st.inc(sync.dma_start(
                out=out.ap().rearrange("(t p) d -> p t d", p=128),
                in_=resid2[:, :].rearrange("p (t d) -> p t d", t=NTT)), 16)
            sync.wait_ge(st.h, st.n)
        ctx.block.sync(_so)

        es2.close()
    return nc, ctx.tap_shapes



def _prep_shared(inputs):
    c = np.ascontiguousarray
    f = lambda k: np.asarray(inputs[k], np.float32)
    wr = c(f("L0_ffn_W_router").reshape(FC, 128, 8).transpose(1, 0, 2))
    wsg = c(f("L0_ffn_Ws_gate").reshape(FC, 128, SF, 128).transpose(2, 1, 0, 3))
    wsu = c(f("L0_ffn_Ws_up").reshape(FC, 128, SF, 128).transpose(2, 1, 0, 3))
    wsd = c(f("L0_ffn_Ws_down").reshape(SF, 128, NOF, 512).transpose(2, 1, 0, 3))
    weg = c(f("L0_ffn_We_gate").reshape(NE, FC, 128, EF, 128).transpose(0, 3, 2, 1, 4))
    weu = c(f("L0_ffn_We_up").reshape(NE, FC, 128, EF, 128).transpose(0, 3, 2, 1, 4))
    wed = c(f("L0_ffn_We_down").reshape(NE, EF, 128, NOF, 512).transpose(0, 3, 2, 1, 4))
    cident = np.eye(128, dtype=np.float32)
    idx = np.arange(128)
    ctriu = ((idx[:, None] <= idx[None, :]) &
             (idx[:, None] // 64 == idx[None, :] // 64)).astype(np.float32)
    cend = (idx[:, None] == ((idx[None, :] // 64) * 64 + 63)).astype(np.float32)
    j = np.arange(64)
    mlt = np.where(j[:, None] < j[None, :], 0.0, -1e9).astype(np.float32)
    mat = np.where(j[:, None] <= j[None, :], 0.0, -1e9).astype(np.float32)
    cmask = np.concatenate([np.concatenate([mlt, mat], 1)] * 2, 0)
    nw = c(f("L0_attn_norm_weight").reshape(128, 1))
    return dict(wr=wr, wsg=wsg, wsu=wsu, wsd=wsd, weg=weg, weu=weu, wed=wed,
                cident=cident, ctriu=ctriu, cend=cend, cmask=cmask, nw=nw)


def _prep_attn(inputs, p):
    c = np.ascontiguousarray
    wq = np.asarray(inputs["L0_attn_W_qkvz"], np.float32)
    wb = np.asarray(inputs["L0_attn_W_ba"], np.float32)
    cw = np.asarray(inputs["L0_attn_conv_weight"], np.float32)
    wqkv = np.empty((NG, 8, D, 128), np.float32)
    wconv = np.empty((NG, 8, 128, 4), np.float32)
    for g in range(NG):
        h0, h1 = 8 * p + 2 * g, 8 * p + 2 * g + 1
        cols = []
        rows = []
        for h in (h0, h1):
            cols.append(wq[:, h * 768:h * 768 + 128])
            rows.append(cw[h * 128:(h + 1) * 128])
        for h in (h0, h1):
            cols.append(wq[:, h * 768 + 128:h * 768 + 256])
            rows.append(cw[2048 + h * 128:2048 + (h + 1) * 128])
        for vh in (2 * h0, 2 * h0 + 1, 2 * h1, 2 * h1 + 1):
            h = vh // 2
            cols.append(wq[:, h * 768 + 256 + (vh % 2) * 128:
                           h * 768 + 256 + (vh % 2 + 1) * 128])
            rows.append(cw[4096 + vh * 128:4096 + (vh + 1) * 128])
        wqkv[g] = np.stack(cols)
        wconv[g] = np.stack(rows)
    wqkv = c(wqkv.reshape(NG, 8 * FC, 128, 128).transpose(0, 2, 1, 3))
    wconv = c(wconv.transpose(0, 2, 1, 3))
    wz = np.empty((NG, D, 512), np.float32)
    for g in range(NG):
        h0, h1 = 8 * p + 2 * g, 8 * p + 2 * g + 1
        zc = []
        for vh in (2 * h0, 2 * h0 + 1, 2 * h1, 2 * h1 + 1):
            h = vh // 2
            zc.append(wq[:, h * 768 + 512 + (vh % 2) * 128:
                         h * 768 + 512 + (vh % 2 + 1) * 128])
        wz[g] = np.concatenate(zc, 1)
    wz = c(wz.reshape(NG, FC, 128, 512).transpose(0, 2, 1, 3))
    wba = c(wb[:, 4 * 8 * p:4 * 8 * (p + 1)].reshape(FC, 128, 32).transpose(1, 0, 2))
    alog = c(np.asarray(inputs["L0_attn_A_log"], np.float32)[16 * p:16 * (p + 1)]
             .reshape(1, 16))
    dtb = c(np.asarray(inputs["L0_attn_dt_bias"], np.float32)[16 * p:16 * (p + 1)]
            .reshape(1, 16))
    wo = np.asarray(inputs["L0_attn_W_out"], np.float32).reshape(32, 128, NOF, 512)
    order = list(range(16 * p, 16 * (p + 1))) + list(range(16 * (1 - p),
                                                           16 * (2 - p)))
    wout = c(wo[order].transpose(2, 1, 0, 3))
    own = [p * TOWN + vg * 128 * S for vg in range(4)]
    par = [(1 - p) * TOWN + vg * 128 * S for vg in range(4)]
    pofs = np.array([[p * TOWN, (1 - p) * TOWN, (1 - p) * 16 * 128 * TOWN, 0]
                     + own + par], dtype=np.int32)
    return dict(wqkv=wqkv, wz=wz, wba=wba, wconv=wconv, alog=alog, dtb=dtb,
                wout=wout, pofs=pofs)


def kernel(**inputs):
    shared = _prep_shared(inputs)
    attn_p = [_prep_attn(inputs, p) for p in range(2)]
    x = np.asarray(inputs["L0_attn_hidden_states"], np.float32)
    nc, tap_shapes = build_nc(_TAPS)
    in_maps = []
    for core in range(8):
        b, p = divmod(core, 2)
        m = dict(shared)
        m.update(attn_p[p])
        m["x0"] = np.ascontiguousarray(x[b, p * TOWN:(p + 1) * TOWN, :])
        in_maps.append(m)
    res = run_bass_kernel_spmd(nc, in_maps, core_ids=list(range(8)))
    outs = np.empty((4, S, D), dtype=np.float32)
    for core in range(8):
        b, p = divmod(core, 2)
        outs[b, p * TOWN:(p + 1) * TOWN, :] = res.results[core]["out"]
    if _TAPS:
        kernel.tap_results = [{k: res.results[c].get(f"tap_{k}") for k in _TAPS}
                              for c in range(8)]
    return outs
